# version 01 (final)
# baseline (speedup 1.0000x reference)
import numpy as np
import concourse.bass as bass
import concourse.mybir as mybir
from concourse.alu_op_type import AluOpType as OP
from concourse.bass_utils import run_bass_kernel_spmd
from contextlib import ExitStack

F32 = mybir.dt.float32
BF16 = mybir.dt.bfloat16
I32 = mybir.dt.int32
AF = mybir.ActivationFunctionType
AX = mybir.AxisListType.X

D = 1024
DI = 2048
HF = 3072
T = 4096
LT = 512
NT = T // LT
NTS = LT // 128
KD = D // 128
KC = DI // 128
KH = HF // 128
INV_D = 1.0 / 1024.0
EPS = 1e-5
QS = 1.0 / 127.0


def build_nc():
    nc = bass.Bass("TRN2", target_bir_lowering=False, detect_race_conditions=False)

    x0_d = nc.dram_tensor("x0", [T, D], F32, kind="ExternalInput").ap()
    w_in_d = nc.dram_tensor("w_in", [DI, D], F32, kind="ExternalInput").ap()
    w_out_t_d = nc.dram_tensor("w_out_t", [DI, D], F32, kind="ExternalInput").ap()
    wg_d = nc.dram_tensor("wg", [HF, D], I32, kind="ExternalInput").ap()
    wu_d = nc.dram_tensor("wu", [HF, D], I32, kind="ExternalInput").ap()
    wd_t_d = nc.dram_tensor("wd_t", [HF, D], I32, kind="ExternalInput").ap()
    cw_d = nc.dram_tensor("cw", [128, KC * 4], F32, kind="ExternalInput").ap()
    cb_d = nc.dram_tensor("cb", [128, KC], F32, kind="ExternalInput").ap()
    ident_d = nc.dram_tensor("ident", [128, 128], F32, kind="ExternalInput").ap()
    consts_d = nc.dram_tensor("consts", [128, 8], F32, kind="ExternalInput").ap()
    out_d = nc.dram_tensor("out", [T, D], F32, kind="ExternalOutput").ap()
    sc_in_d = nc.dram_tensor("sc_in", [KC, 2, 128, D], BF16, kind="Internal").ap()
    sc_out_d = nc.dram_tensor("sc_out", [KC, 2, 128, D], BF16, kind="Internal").ap()
    sc_gu_d = nc.dram_tensor("sc_gu", [KH, 2, 128, D], BF16, kind="Internal").ap()
    sc_d_d = nc.dram_tensor("sc_d", [KH, 128, D], BF16, kind="Internal").ap()

    es = ExitStack()
    sb = lambda n, sh, dt: es.enter_context(nc.sbuf_tensor(n, sh, dt))

    XB = [sb("xb0", [128, NTS * D], F32), sb("xb1", [128, NTS * D], F32)]
    HC = sb("hc", [128, D], F32)
    H1 = [sb("h1a", [128, D], F32), sb("h1b", [128, D], F32)]
    SQT = sb("sqt", [128, D], F32)
    hiT = sb("hiT", [128, KD * LT], BF16)
    loT = sb("loT", [128, KD * LT], BF16)
    fp_hi = sb("fp_hi", [128, KH * LT], BF16)
    fp_lo = sb("fp_lo", [128, KH * LT], BF16)
    CONVA = [sb("cva0", [128, LT], F32), sb("cva1", [128, LT], F32)]
    XCF = [sb("xcf0", [128, LT], F32), sb("xcf1", [128, LT], F32)]
    HALO = sb("halo", [128, KC * 3], F32)
    CW = sb("cwt", [128, KC * 4], F32)
    CB = sb("cbt", [128, KC], F32)
    GS = [sb("gs0", [128, LT], F32), sb("gs1", [128, LT], F32)]
    FFT = [sb("fft0", [128, LT], F32), sb("fft1", [128, LT], F32)]
    SUMS = sb("sums", [128, NTS], F32)
    MEANS = sb("means", [128, NTS], F32)
    SS = sb("ssq", [128, NTS], F32)
    STD = sb("stdt", [128, NTS], F32)
    RSTD = sb("rstd", [128, NTS], F32)
    WIN = [sb("win0", [128, 2048], BF16), sb("win1", [128, 2048], BF16)]
    WOUT = [sb("wout0", [128, 2048], BF16), sb("wout1", [128, 2048], BF16)]
    WGU = [sb("wgu0", [128, 2048], BF16), sb("wgu1", [128, 2048], BF16)]
    WDN = [sb("wdn0", [128, D], BF16), sb("wdn1", [128, D], BF16)]
    PIN = [sb("pin0", [128, D], F32), sb("pin1", [128, D], F32)]
    PINT = [sb("pint0", [128, D], I32), sb("pint1", [128, D], I32)]
    PBH = [sb("pbh0", [128, D], BF16), sb("pbh1", [128, D], BF16)]
    PBL = [sb("pbl0", [128, D], BF16), sb("pbl1", [128, D], BF16)]
    IDS = sb("ids", [128, 128], F32)
    CST = sb("cst", [128, 8], F32)
    nc.const_aps.register(F32, INV_D, CST[:, 0:1])
    nc.const_aps.register(F32, EPS, CST[:, 1:2])
    nc.const_aps.register(F32, QS, CST[:, 2:3])
    nc.const_aps.register(F32, 0.0, CST[:, 3:4])
    nc.const_aps.register(F32, 1.0, CST[:, 4:5])

    QP = [es.enter_context(nc.psum_tensor(f"qp{b}", [128, 512], F32)) for b in range(8)]

    sems = {k: es.enter_context(nc.semaphore(k)) for k in ("w", "g", "pe", "v", "s")}

    prog = {k: [] for k in ("sync", "gpsimd", "vector", "scalar", "tensor")}
    cnt = {k: 0 for k in sems}

    def em(eng, fn):
        prog[eng].append(fn)

    def inc(eng, mk, sem, n=1):
        cnt[sem] += n
        em(eng, lambda e, m=mk, s=sems[sem], nn=n: m(e).then_inc(s, nn))
        return cnt[sem]

    def dma(eng, out_ap, in_ap, sem):
        return inc(eng, lambda e, o=out_ap, i=in_ap: e.dma_start(out=o, in_=i), sem, 16)

    def wait(eng, sem, v):
        if v:
            em(eng, lambda e, s=sems[sem], vv=v: e.wait_ge(s, vv))

    def op(eng, mk):
        em(eng, lambda e, m=mk: m(e))

    def mm_group(mms, sem="pe"):
        for n, (o, l, r, st, sp) in enumerate(mms):
            mk = lambda e, oo=o, ll=l, rr=r, s1=st, s2=sp: e.matmul(
                oo(), ll(), rr(), start=s1, stop=s2)
            if n == len(mms) - 1:
                return inc("tensor", mk, sem)
            op("tensor", mk)

    dma("sync", IDS[:], ident_d, "w")
    dma("sync", CW[:], cw_d, "w")
    dma("sync", CB[:], cb_d, "w")
    e_cst = dma("sync", CST[:], consts_d, "w")
    x0_evt = [None] * NT
    for i in range(2):
        x0_evt[i] = dma(
            "gpsimd", XB[i][:],
            x0_d[i * LT:(i + 1) * LT, :].rearrange("(s p) d -> p s d", p=128), "g")
    op("vector", lambda e: e.memset(HALO[:], 0.0))
    wait("tensor", "w", e_cst)
    e_dummy = inc("tensor", lambda e: e.matmul(QP[0][:, 0:1], IDS[:], IDS[:, 0:1],
                                               start=True, stop=True), "pe")

    bank_free = {b: ("pe", e_dummy) if b == 0 else None for b in range(8)}

    def wait_banks(banks):
        best = {}
        for b in banks:
            bf = bank_free[b]
            if bf is not None:
                s, v = bf
                if best.get(s, 0) < v:
                    best[s] = v
        for s, v in best.items():
            wait("tensor", s, v)
        for b in banks:
            bank_free[b] = None

    pin_free = [None, None]
    pint_free = [None, None]
    pb_free = [None, None]
    prep_wr = {"in": [], "out": [], "gu": [], "d": []}

    def prep_transpose_evac(j, psum_banks, split):
        for k in range(KD):
            bank = psum_banks[k // 4]
            mk = lambda e, b=bank, kk=k, jj=j: e.transpose(
                b[:, (kk % 4) * 128:(kk % 4 + 1) * 128],
                PIN[jj][:, kk * 128:(kk + 1) * 128], IDS[:])
            if k == KD - 1:
                e_tr = inc("tensor", mk, "pe")
            else:
                op("tensor", mk)
        wait("vector", "pe", e_tr)
        if pb_free[j] is not None:
            wait("vector", "g", pb_free[j])
        for k in range(KD):
            bank = psum_banks[k // 4]
            bs = (k % 4) * 128
            op("vector", lambda e, b=bank, o=k * 128, s=bs, jj=j: e.tensor_copy(
                out=PBH[jj][:, o:o + 128], in_=b[:, s:s + 128]))
            mk = lambda e, b=bank, o=k * 128, s=bs, jj=j: e.tensor_tensor(
                out=PBL[jj][:, o:o + 128], in0=b[:, s:s + 128],
                in1=PBH[jj][:, o:o + 128], op=OP.subtract)
            if split:
                if k == KD - 1:
                    e_ev = inc("vector", mk, "v")
                else:
                    op("vector", mk)
            elif k == KD - 1:
                pass
        if not split:
            e_ev = inc("vector", lambda e, b=psum_banks[1], jj=j: e.tensor_copy(
                out=PBH[jj][:, (KD - 1) * 128:(KD - 1) * 128 + 128],
                in_=b[:, 3 * 128:4 * 128]), "v")
        return e_tr, e_ev

    for c in range(KC):
        j = c % 2
        if pin_free[j] is not None:
            wait("sync", pin_free[j][0], pin_free[j][1])
        e_ld = dma("sync", PIN[j][:], w_in_d[c * 128:(c + 1) * 128, :], "w")
        banks = (QP[4], QP[5]) if j == 0 else (QP[6], QP[7])
        wait("tensor", "w", e_ld)
        wait_banks((4, 5) if j == 0 else (6, 7))
        e_tr, e_ev = prep_transpose_evac(j, banks, split=True)
        pin_free[j] = ("pe", e_tr)
        for b in ((4, 5) if j == 0 else (6, 7)):
            bank_free[b] = ("v", e_ev)
        wait("gpsimd", "v", e_ev)
        dma("gpsimd", sc_in_d[c, 0], PBH[j][:], "g")
        e_wr = dma("gpsimd", sc_in_d[c, 1], PBL[j][:], "g")
        pb_free[j] = e_wr
        prep_wr["in"].append(e_wr)

    for k in range(KC):
        j = k % 2
        if pin_free[j] is not None:
            wait("sync", pin_free[j][0], pin_free[j][1])
        e_ld = dma("sync", PIN[j][:], w_out_t_d[k * 128:(k + 1) * 128, :], "w")
        wait("vector", "w", e_ld)
        if pb_free[j] is not None:
            wait("vector", "g", pb_free[j])
        op("vector", lambda e, jj=j: e.tensor_copy(out=PBH[jj][:], in_=PIN[jj][:]))
        e_ev = inc("vector", lambda e, jj=j: e.tensor_tensor(
            out=PBL[jj][:], in0=PIN[jj][:], in1=PBH[jj][:], op=OP.subtract), "v")
        pin_free[j] = ("v", e_ev)
        wait("gpsimd", "v", e_ev)
        dma("gpsimd", sc_out_d[k, 0], PBH[j][:], "g")
        e_wr = dma("gpsimd", sc_out_d[k, 1], PBL[j][:], "g")
        pb_free[j] = e_wr
        prep_wr["out"].append(e_wr)

    inst = 0
    for h in range(KH):
        for which, src in ((0, wg_d), (1, wu_d)):
            j = inst % 2
            if pint_free[j] is not None:
                wait("sync", pint_free[j][0], pint_free[j][1])
            e_ld = dma("sync", PINT[j][:], src[h * 128:(h + 1) * 128, :], "w")
            wait("vector", "w", e_ld)
            if pin_free[j] is not None:
                wait("vector", pin_free[j][0], pin_free[j][1])
            e_cast = inc("vector", lambda e, jj=j: e.tensor_copy(
                out=PIN[jj][:], in_=PINT[jj][:]), "v")
            pint_free[j] = ("v", e_cast)
            banks = (QP[4], QP[5]) if j == 0 else (QP[6], QP[7])
            wait("tensor", "v", e_cast)
            wait_banks((4, 5) if j == 0 else (6, 7))
            e_tr, e_ev = prep_transpose_evac(j, banks, split=False)
            pin_free[j] = ("pe", e_tr)
            for b in ((4, 5) if j == 0 else (6, 7)):
                bank_free[b] = ("v", e_ev)
            wait("gpsimd", "v", e_ev)
            e_wr = dma("gpsimd", sc_gu_d[h, which], PBH[j][:], "g")
            pb_free[j] = e_wr
            prep_wr["gu"].append(e_wr)
            inst += 1

    for k in range(KH):
        j = k % 2
        if pint_free[j] is not None:
            wait("sync", pint_free[j][0], pint_free[j][1])
        e_ld = dma("sync", PINT[j][:], wd_t_d[k * 128:(k + 1) * 128, :], "w")
        wait("vector", "w", e_ld)
        if pb_free[j] is not None:
            wait("vector", "g", pb_free[j])
        e_ev = inc("vector", lambda e, jj=j: e.tensor_copy(
            out=PBH[jj][:], in_=PINT[jj][:]), "v")
        pint_free[j] = ("v", e_ev)
        wait("gpsimd", "v", e_ev)
        e_wr = dma("gpsimd", sc_d_d[k], PBH[j][:], "g")
        pb_free[j] = e_wr
        prep_wr["d"].append(e_wr)

    consumed = {"in": [], "out": [], "gu": [], "dn": []}

    def ln_transpose(xb):
        evac_evt = [None] * NTS
        for ts in range(NTS):
            sl = lambda buf=xb, t=ts: buf[:, t * D:(t + 1) * D]
            op("vector", lambda e, s=sl, t=ts: e.reduce_sum(
                out=SUMS[:, t:t + 1], in_=s(), axis=AX))
            op("vector", lambda e: e.drain())
            op("vector", lambda e, t=ts: e.tensor_scalar_mul(
                MEANS[:, t:t + 1], SUMS[:, t:t + 1], INV_D))
            op("vector", lambda e: e.drain())
            e_hc = inc("vector", lambda e, s=sl, t=ts: e.tensor_scalar_sub(
                HC[:], s(), MEANS[:, t:t + 1]), "v")
            wait("scalar", "v", e_hc)
            op("scalar", lambda e, t=ts: e.activation(
                out=SQT[:], in_=HC[:], func=AF.Square, accum_out=SS[:, t:t + 1]))
            op("scalar", lambda e: e.drain())
            e_std = inc("scalar", lambda e, t=ts: e.activation(
                out=STD[:, t:t + 1], in_=SS[:, t:t + 1], func=AF.Sqrt,
                scale=INV_D, bias=EPS), "s")
            wait("vector", "s", e_std)
            op("vector", lambda e, t=ts: e.reciprocal(RSTD[:, t:t + 1], STD[:, t:t + 1]))
            op("vector", lambda e: e.drain())
            hb = H1[ts % 2]
            e_h1 = inc("vector", lambda e, t=ts, b=hb: e.tensor_scalar_mul(
                b[:], HC[:], RSTD[:, t:t + 1]), "v")
            p = ts % 2
            pbanks = (0, 1) if p == 0 else (2, 3)
            wait("tensor", "v", e_h1)
            wait_banks(pbanks)
            for k in range(KD):
                bank = QP[pbanks[k // 4]]
                mk = lambda e, b=bank, kk=k, hbb=hb: e.transpose(
                    b[:, (kk % 4) * 128:(kk % 4 + 1) * 128],
                    hbb[:, kk * 128:(kk + 1) * 128], IDS[:])
                if k == KD - 1:
                    e_tr = inc("tensor", mk, "pe")
                else:
                    op("tensor", mk)
            wait("vector", "pe", e_tr)
            for k in range(KD):
                bank = QP[pbanks[k // 4]]
                bs = (k % 4) * 128
                co = k * LT + ts * 128
                op("vector", lambda e, b=bank, s=bs, c=co: e.tensor_copy(
                    out=hiT[:, c:c + 128], in_=b[:, s:s + 128]))
                mk = lambda e, b=bank, s=bs, c=co: e.tensor_tensor(
                    out=loT[:, c:c + 128], in0=b[:, s:s + 128],
                    in1=hiT[:, c:c + 128], op=OP.subtract)
                if k == KD - 1:
                    e_ev = inc("vector", mk, "v")
                else:
                    op("vector", mk)
            evac_evt[ts] = e_ev
            for b in pbanks:
                bank_free[b] = ("v", e_ev)
        return evac_evt

    for i in range(NT):
        xb = XB[i % 2]
        wait("vector", "g", x0_evt[i])
        a_evac = ln_transpose(xb)

        conv_evt = [None] * KC
        xcsplit_evt = [None] * KC
        for c in range(KC):
            j = c % 2
            idx = len(consumed["in"])
            if idx >= 2:
                wait("sync", "pe", consumed["in"][idx - 2])
            if i == 0:
                wait("sync", "g", prep_wr["in"][c])
            e_w = dma("sync", WIN[j][:], sc_in_d[c].rearrange("hl p f -> p hl f"), "w")
            bnum = 4 + j
            bank = QP[bnum]
            wait("tensor", "w", e_w)
            if c == 0:
                wait("tensor", "v", a_evac[NTS - 1])
            wait_banks((bnum,))
            mms = []
            for k in range(KD):
                for (wofs, rT) in ((0, hiT), (0, loT), (D, hiT)):
                    mms.append((
                        lambda b=bank: b[:],
                        lambda jj=j, kk=k, wo=wofs: WIN[jj][:, wo + kk * 128: wo + kk * 128 + 128],
                        lambda r=rT, kk=k: r[:, kk * LT:(kk + 1) * LT],
                        len(mms) == 0, len(mms) == 23))
            e_mm = mm_group(mms)
            consumed["in"].append(e_mm)
            wait("vector", "pe", e_mm)
            ca = CONVA[j]
            op("vector", lambda e, b=bank, a=ca, cc=c: e.tensor_scalar_mul(
                a[:], b[:], CW[:, cc * 4 + 3: cc * 4 + 4]))
            for (jt, sh) in ((2, 1), (1, 2), (0, 3)):
                op("vector", lambda e, b=bank, a=ca, cc=c, j2=jt, s2=sh: e.scalar_tensor_tensor(
                    out=a[:, s2:LT], in0=b[:, 0:LT - s2],
                    scalar=CW[:, cc * 4 + j2: cc * 4 + j2 + 1],
                    in1=a[:, s2:LT], op0=OP.mult, op1=OP.add))
            op("vector", lambda e, a=ca, cc=c: e.scalar_tensor_tensor(
                out=a[:, 0:1], in0=HALO[:, cc * 3 + 2: cc * 3 + 3],
                scalar=CW[:, cc * 4 + 2: cc * 4 + 3],
                in1=a[:, 0:1], op0=OP.mult, op1=OP.add))
            op("vector", lambda e, a=ca, cc=c: e.scalar_tensor_tensor(
                out=a[:, 0:2], in0=HALO[:, cc * 3 + 1: cc * 3 + 3],
                scalar=CW[:, cc * 4 + 1: cc * 4 + 2],
                in1=a[:, 0:2], op0=OP.mult, op1=OP.add))
            op("vector", lambda e, a=ca, cc=c: e.scalar_tensor_tensor(
                out=a[:, 0:3], in0=HALO[:, cc * 3: cc * 3 + 3],
                scalar=CW[:, cc * 4: cc * 4 + 1],
                in1=a[:, 0:3], op0=OP.mult, op1=OP.add))
            e_cv = inc("vector", lambda e, b=bank, cc=c: e.tensor_copy(
                out=HALO[:, cc * 3: cc * 3 + 3], in_=b[:, LT - 3:LT]), "v")
            conv_evt[c] = e_cv
            bank_free[bnum] = ("v", e_cv)
            wait("scalar", "v", e_cv)
            xf = XCF[j]
            e_si = inc("scalar", lambda e, a=ca, x=xf, cc=c: e.activation(
                out=x[:], in_=a[:], func=AF.Silu, bias=CB[:, cc:cc + 1]), "s")
            wait("vector", "s", e_si)
            co = c * LT
            op("vector", lambda e, x=xf, cc=co: e.tensor_copy(
                out=fp_hi[:, cc:cc + LT], in_=x[:]))
            e_sp = inc("vector", lambda e, x=xf, cc=co: e.tensor_tensor(
                out=fp_lo[:, cc:cc + LT], in0=x[:], in1=fp_hi[:, cc:cc + LT],
                op=OP.subtract), "v")
            xcsplit_evt[c] = e_sp

        for k in range(KC):
            j = k % 2
            idx = len(consumed["out"])
            if idx >= 2:
                wait("sync", "pe", consumed["out"][idx - 2])
            if i == 0:
                wait("sync", "g", prep_wr["out"][k])
            e_w = dma("sync", WOUT[j][:], sc_out_d[k].rearrange("hl p f -> p hl f"), "w")
            wait("tensor", "w", e_w)
            wait("tensor", "v", xcsplit_evt[k])
            if k == 0:
                wait_banks(tuple(range(8)))
            mms = []
            for ts in range(NTS):
                for (side, w0) in (("hi", 0), ("hi", 512), ("hi", 1024),
                                   ("hi", 1536), ("lo", 0), ("lo", 512)):
                    lt = fp_hi if side == "hi" else fp_lo
                    half = (w0 // 512) % 2
                    b = ts * 2 + half
                    st = (k == 0) and (side == "hi") and (w0 < 1024)
                    sp = (k == KC - 1) and (side == "lo")
                    mms.append((
                        lambda bb=QP[b]: bb[:],
                        lambda l=lt, kk=k, t=ts: l[:, kk * LT + t * 128: kk * LT + t * 128 + 128],
                        lambda jj=j, ww=w0: WOUT[jj][:, ww:ww + 512],
                        st, sp))
            e_mm = mm_group(mms)
            consumed["out"].append(e_mm)
        e_outproj = consumed["out"][-1]

        wait("vector", "pe", e_outproj)
        h_pair = [None] * NTS
        for ts in range(NTS):
            for half in range(2):
                b = ts * 2 + half
                co = ts * D + half * 512
                mk = lambda e, bb=QP[b], c=co, x=xb: e.tensor_tensor(
                    out=x[:, c:c + 512], in0=bb[:], in1=x[:, c:c + 512], op=OP.add)
                if half == 1:
                    e_y = inc("vector", mk, "v")
                else:
                    op("vector", mk)
            h_pair[ts] = e_y
            bank_free[ts * 2] = ("v", e_y)
            bank_free[ts * 2 + 1] = ("v", e_y)
        e_evac = ln_transpose(xb)

        gsilu_evt = [None] * KH
        fsplit_evt = [None] * KH
        for h in range(KH):
            j = h % 2
            idx = len(consumed["gu"])
            if idx >= 2:
                wait("sync", "pe", consumed["gu"][idx - 2])
            if i == 0:
                wait("sync", "g", prep_wr["gu"][2 * h + 1])
            e_w = dma("sync", WGU[j][:], sc_gu_d[h].rearrange("hl p f -> p hl f"), "w")
            gb = 4 + j
            ub = 6 + j
            wait("tensor", "w", e_w)
            if h == 0:
                wait("tensor", "v", e_evac[NTS - 1])
            wait_banks((gb,))
            mms = []
            for k in range(KD):
                for rT in (hiT, loT):
                    mms.append((
                        lambda b=QP[gb]: b[:],
                        lambda jj=j, kk=k: WGU[jj][:, kk * 128: kk * 128 + 128],
                        lambda r=rT, kk=k: r[:, kk * LT:(kk + 1) * LT],
                        len(mms) == 0, len(mms) == 15))
            e_g = mm_group(mms)
            wait_banks((ub,))
            mms = []
            for k in range(KD):
                for rT in (hiT, loT):
                    mms.append((
                        lambda b=QP[ub]: b[:],
                        lambda jj=j, kk=k: WGU[jj][:, D + kk * 128: D + kk * 128 + 128],
                        lambda r=rT, kk=k: r[:, kk * LT:(kk + 1) * LT],
                        len(mms) == 0, len(mms) == 15))
            e_u = mm_group(mms)
            consumed["gu"].append(e_u)
            wait("scalar", "pe", e_g)
            if h >= 2:
                wait("scalar", "v", fsplit_evt[h - 2])
            e_si = inc("scalar", lambda e, b=QP[gb], g=GS[j]: e.activation(
                out=g[:], in_=b[:], func=AF.Silu, scale=QS), "s")
            gsilu_evt[h] = e_si
            bank_free[gb] = ("s", e_si)
            wait("vector", "pe", e_u)
            wait("vector", "s", e_si)
            ff = FFT[j]
            op("vector", lambda e, b=QP[ub], g=GS[j], f=ff: e.scalar_tensor_tensor(
                out=f[:], in0=b[:], scalar=QS, in1=g[:], op0=OP.mult, op1=OP.mult))
            co = h * LT
            op("vector", lambda e, f=ff, cc=co: e.tensor_copy(
                out=fp_hi[:, cc:cc + LT], in_=f[:]))
            e_sp = inc("vector", lambda e, f=ff, cc=co: e.tensor_tensor(
                out=fp_lo[:, cc:cc + LT], in0=f[:], in1=fp_hi[:, cc:cc + LT],
                op=OP.subtract), "v")
            fsplit_evt[h] = e_sp
            bank_free[ub] = ("v", e_sp)

        for k in range(KH):
            j = k % 2
            idx = len(consumed["dn"])
            if idx >= 2:
                wait("sync", "pe", consumed["dn"][idx - 2])
            if i == 0:
                wait("sync", "g", prep_wr["d"][k])
            e_w = dma("sync", WDN[j][:], sc_d_d[k], "w")
            wait("tensor", "w", e_w)
            wait("tensor", "v", fsplit_evt[k])
            if k == 0:
                wait_banks(tuple(range(8)))
            mms = []
            for ts in range(NTS):
                for (side, w0) in (("hi", 0), ("hi", 512), ("lo", 0), ("lo", 512)):
                    lt = fp_hi if side == "hi" else fp_lo
                    half = w0 // 512
                    b = ts * 2 + half
                    st = (k == 0) and (side == "hi")
                    sp = (k == KH - 1) and (side == "lo")
                    mms.append((
                        lambda bb=QP[b]: bb[:],
                        lambda l=lt, kk=k, t=ts: l[:, kk * LT + t * 128: kk * LT + t * 128 + 128],
                        lambda jj=j, ww=w0: WDN[jj][:, ww:ww + 512],
                        st, sp))
            e_mm = mm_group(mms)
            consumed["dn"].append(e_mm)
        e_down = consumed["dn"][-1]

        wait("vector", "pe", e_down)
        h_pair_out = [None] * NTS
        for ts in range(NTS):
            for half in range(2):
                b = ts * 2 + half
                co = ts * D + half * 512
                mk = lambda e, bb=QP[b], c=co, x=xb: e.scalar_tensor_tensor(
                    out=x[:, c:c + 512], in0=bb[:], scalar=QS, in1=x[:, c:c + 512],
                    op0=OP.mult, op1=OP.add)
                if half == 1:
                    e_o = inc("vector", mk, "v")
                else:
                    op("vector", mk)
            h_pair_out[ts] = e_o
            bank_free[ts * 2] = ("v", e_o)
            bank_free[ts * 2 + 1] = ("v", e_o)
        e_tile = h_pair_out[NTS - 1]
        wait("gpsimd", "v", e_tile)
        store_e = dma(
            "gpsimd", out_d[i * LT:(i + 1) * LT, :].rearrange("(s p) d -> p s d", p=128),
            xb[:], "g")
        if i + 2 < NT:
            x0_evt[i + 2] = dma(
                "gpsimd", XB[i % 2][:],
                x0_d[(i + 2) * LT:(i + 3) * LT, :].rearrange("(s p) d -> p s d", p=128), "g")

    em("gpsimd", lambda e, s=sems["g"], v=cnt["g"]: e.wait_ge(s, v))

    with nc.Block() as block:
        @block.sync
        def _(e):
            for f in prog["sync"]:
                f(e)

        @block.gpsimd
        def _(e):
            for f in prog["gpsimd"]:
                f(e)

        @block.vector
        def _(e):
            for f in prog["vector"]:
                f(e)

        @block.scalar
        def _(e):
            for f in prog["scalar"]:
                f(e)

        @block.tensor
        def _(e):
            for f in prog["tensor"]:
                f(e)

    es.close()
    return nc


_NC_CACHE = None


def kernel(L0_attn_hidden_states, L0_attn_in_proj_w, L0_attn_conv_w, L0_attn_conv_b,
           L0_attn_x_proj_w, L0_attn_dt_proj_w, L0_attn_dt_proj_b, L0_attn_A_log,
           L0_attn_D, L0_attn_out_proj_w, L0_ffn_w_gate, L0_ffn_w_up, L0_ffn_w_down):
    global _NC_CACHE
    x = np.ascontiguousarray(np.asarray(L0_attn_hidden_states, dtype=np.float32))
    w_in = np.ascontiguousarray(np.asarray(L0_attn_in_proj_w, dtype=np.float32)[:DI])
    w_out_t = np.ascontiguousarray(np.asarray(L0_attn_out_proj_w, dtype=np.float32).T)
    wg = np.ascontiguousarray(np.asarray(L0_ffn_w_gate, dtype=np.int32))
    wu = np.ascontiguousarray(np.asarray(L0_ffn_w_up, dtype=np.int32))
    wd_t = np.ascontiguousarray(np.asarray(L0_ffn_w_down, dtype=np.int32).T)
    conv_w = np.asarray(L0_attn_conv_w, dtype=np.float32).reshape(DI, 4)
    cw = np.ascontiguousarray(conv_w.reshape(KC, 128, 4).transpose(1, 0, 2).reshape(128, KC * 4))
    cb = np.ascontiguousarray(np.asarray(L0_attn_conv_b, dtype=np.float32).reshape(KC, 128).T)
    ident = np.eye(128, dtype=np.float32)
    consts = np.tile(np.array([[INV_D, EPS, QS, 0.0, 1.0, 0.0, 0.0, 0.0]],
                              dtype=np.float32), (128, 1))
    if _NC_CACHE is None:
        _NC_CACHE = build_nc()
    nc = _NC_CACHE
    in_maps = []
    for b in range(8):
        in_maps.append({
            "x0": np.ascontiguousarray(x[b]),
            "w_in": w_in, "w_out_t": w_out_t,
            "wg": wg, "wu": wu, "wd_t": wd_t,
            "cw": cw, "cb": cb, "ident": ident, "consts": consts,
        })
    res = run_bass_kernel_spmd(nc, in_maps, core_ids=list(range(8)))
    return np.stack([np.asarray(res.results[b]["out"]) for b in range(8)], axis=0)
